# version 08 (final)
# speedup vs baseline: 1.3636x; 1.0039x over previous
import math
import numpy as np
from contextlib import ExitStack

import concourse.bass as bass
import concourse.mybir as mybir
from concourse.bass_utils import run_bass_kernel_spmd

F32 = mybir.dt.float32
BF16 = mybir.dt.bfloat16
AF = mybir.ActivationFunctionType
ALU = mybir.AluOpType
AX = mybir.AxisListType

B, L, D = 2, 4096, 2048
NCORES = 8
RPC = L // 4
DSL = D // 4
DT_RANK = 128
N_STATE = 16
E_DBL = DT_RANK + 2 * N_STATE
NT = RPC // 128
NK = D // 128
EPS = 1e-5


class Ctx:

    def __init__(self, nc):
        self.nc = nc
        self.es = ExitStack()
        self.sems = {}
        self.cnt = {}

    def sem(self, name):
        if name not in self.sems:
            self.sems[name] = self.es.enter_context(self.nc.semaphore(name))
            self.cnt[name] = 0
        return self.sems[name]

    def inc(self, inst, name, by=1):
        inst.then_inc(self.sem(name), by)
        self.cnt[name] += by
        return self.cnt[name]

    def wait(self, eng, name, val=None):
        eng.wait_ge(self.sem(name), self.cnt[name] if val is None else val)

    def sb(self, name, shape, dtype):
        return self.es.enter_context(self.nc.sbuf_tensor(name, shape, dtype))

    def ps(self, name, shape, dtype):
        return self.es.enter_context(self.nc.psum_tensor(name, shape, dtype))


def emit_ln_stats_apply(ctx, blk, xt, xnf, stat, sp1):
    nc = ctx.nc
    vpre = ctx.cnt["v"]

    @blk.vector
    def _(v):
        ctx.wait(v, "dma")
        ctx.inc(v.tensor_reduce(out=stat[:, 0:1], in_=xt[:], axis=AX.X,
                                op=ALU.add), "v")

    @blk.scalar
    def _(sc):
        ctx.wait(sc, "dma")
        sc.wait_ge(ctx.sems["v"], vpre)
        ctx.inc(sc.activation(out=sp1[:], in_=xt[:], func=AF.Square), "s")

    sv = ctx.cnt["s"]

    @blk.vector
    def _(v):
        v.wait_ge(ctx.sems["s"], sv)
        ctx.inc(v.tensor_reduce(out=stat[:, 1:2], in_=sp1[:], axis=AX.X,
                                op=ALU.add), "v")
        ctx.wait(v, "v")
        v.tensor_scalar(out=stat[:, 2:3], in0=stat[:, 0:1], scalar1=1.0 / D,
                        scalar2=None, op0=ALU.mult)
        ctx.inc(v.tensor_scalar(out=stat[:, 3:4], in0=stat[:, 1:2],
                                scalar1=1.0 / D, scalar2=None, op0=ALU.mult),
                "v")
        ctx.wait(v, "v")
        ctx.inc(v.tensor_copy(out=stat[:, 9:10], in_=stat[:, 2:3]), "v")
        ctx.wait(v, "v")
        ctx.inc(v.tensor_tensor(out=stat[:, 4:5], in0=stat[:, 2:3],
                                in1=stat[:, 9:10], op=ALU.mult), "v")
        ctx.wait(v, "v")
        ctx.inc(v.tensor_tensor(out=stat[:, 5:6], in0=stat[:, 3:4],
                                in1=stat[:, 4:5], op=ALU.subtract), "v")

    vv = ctx.cnt["v"]

    @blk.scalar
    def _(sc):
        sc.wait_ge(ctx.sems["v"], vv)
        ctx.inc(sc.activation(out=stat[:, 6:7], in_=stat[:, 5:6], func=AF.Sqrt,
                              bias=EPS), "s")

    sv2 = ctx.cnt["s"]

    @blk.vector
    def _(v):
        v.wait_ge(ctx.sems["s"], sv2)
        ctx.inc(v.reciprocal(out=stat[:, 7:8], in_=stat[:, 6:7]), "v")
        ctx.wait(v, "v")
        ctx.inc(v.tensor_scalar(out=stat[:, 8:9], in0=stat[:, 2:3],
                                scalar1=stat[:, 7:8], scalar2=-1.0,
                                op0=ALU.mult, op1=ALU.mult), "v")
        ctx.wait(v, "v")
        ctx.inc(v.tensor_scalar(out=xnf[:], in0=xt[:], scalar1=stat[:, 7:8],
                                scalar2=stat[:, 8:9], op0=ALU.mult,
                                op1=ALU.add), "v")


def build_launch_A():
    nc = bass.Bass("TRN2", target_bir_lowering=False, debug=False,
                   num_devices=NCORES)
    ctx = Ctx(nc)

    x = nc.dram_tensor("x", [RPC, D], F32, kind="ExternalInput").ap()
    xpT = nc.dram_tensor("xpT", [D, E_DBL], F32, kind="ExternalInput").ap()
    dtT = nc.dram_tensor("dtT", [DT_RANK, D], F32, kind="ExternalInput").ap()
    dtb = nc.dram_tensor("dtb", [1, D], F32, kind="ExternalInput").ap()
    w0T = nc.dram_tensor("w0T", [D, D], F32, kind="ExternalInput").ap()
    b0 = nc.dram_tensor("b0", [1, D], F32, kind="ExternalInput").ap()
    dpar = nc.dram_tensor("dpar", [1, D], F32, kind="ExternalInput").ap()
    alpha_d = nc.dram_tensor("alpha_d", [1, D], F32, kind="Internal").ap()

    o_delta = nc.dram_tensor("o_delta", [RPC, D], F32, kind="ExternalOutput").ap()
    o_u = nc.dram_tensor("o_u", [RPC, D], F32, kind="ExternalOutput").ap()
    o_bc = nc.dram_tensor("o_bc", [2 * N_STATE, RPC], F32, kind="ExternalOutput").ap()
    o_g0 = nc.dram_tensor("o_g0", [RPC, D], F32, kind="ExternalOutput").ap()
    o_dxn = nc.dram_tensor("o_dxn", [RPC, D], F32, kind="ExternalOutput").ap()

    xnTbf = ctx.sb("xnTbf", [128, NK * 128], BF16)
    sw0 = ctx.sb("sw0", [128, NK * D], BF16)
    xpTbf = ctx.sb("xpTbf", [128, NK * E_DBL], BF16)
    dtTbf = ctx.sb("dtTbf", [128, D], BF16)
    alrow = ctx.sb("alrow", [1, D], F32)
    alphabc = ctx.sb("alphabc", [128, D], F32)
    b0bc = ctx.sb("b0bc", [128, D], F32)
    dtbbc = ctx.sb("dtbbc", [128, D], F32)
    dbc = ctx.sb("dbc", [128, D], F32)
    ident_bf = ctx.sb("ident_bf", [128, 128], BF16)
    ones_bf = ctx.sb("ones_bf", [128, 1], BF16)
    xt = ctx.sb("xt", [128, D], F32)
    absw = ctx.sb("absw", [128, D], BF16)
    absw2 = ctx.sb("absw2", [128, D], BF16)
    xnf = ctx.sb("xnf", [128, D], F32)
    xnbf_tm = ctx.sb("xnbf_tm", [128, D], BF16)
    stat = ctx.sb("stat", [128, 16], F32)
    sp1 = ctx.sb("sp1", [128, D], F32)
    sp2 = ctx.sb("sp2", [128, D], F32)
    zb = ctx.sb("zb", [128, D], F32)
    deltat = ctx.sb("deltat", [128, D], F32)
    ut = ctx.sb("ut", [128, D], F32)
    g0t = ctx.sb("g0t", [128, D], F32)
    xdblTbf = ctx.sb("xdblTbf", [128, RPC], BF16)
    bcT = ctx.sb("bcT", [32, RPC], F32)

    consts = ctx.sb("consts", [128, 4], F32)
    ps_tr = ctx.ps("ps_tr", [128, 128], BF16)
    ps_tr2 = ctx.ps("ps_tr2", [128, 128], BF16)
    ps_g = [ctx.ps(f"ps_g{j}", [128, 512], F32) for j in range(4)]
    ps_e = ctx.ps("ps_e", [128, 128], F32)
    ps_f = ctx.ps("ps_f", [32, 128], F32)

    with ctx.es:
        for j, val in enumerate([EPS, -1.0, 1.0, 0.0]):
            nc.const_aps.register(F32, val, consts[:, j:j + 1])
        with nc.Block() as blk:
            @blk.gpsimd
            def _(gp):
                for j, val in enumerate([EPS, -1.0, 1.0, 0.0]):
                    gp.memset(consts[:, j:j + 1], val)
                gp.iota(ident_bf[:], [[1, 128]], base=0, channel_multiplier=-1,
                        allow_small_or_imprecise_dtypes=True)
                gp.tensor_scalar(out=ident_bf[:], in0=ident_bf[:], scalar1=0.0,
                                 scalar2=None, op0=ALU.is_equal)
                for k in range(NK):
                    ctx.inc(gp.dma_start(out=xpTbf[:, k * E_DBL:(k + 1) * E_DBL],
                                         in_=xpT[128 * k:128 * (k + 1), :]),
                            "gdma", 16)
                ctx.inc(gp.dma_start(out=dtTbf[:], in_=dtT), "gdma", 16)

            @blk.sync
            def _(sync):
                ctx.inc(sync.dma_start(out=b0bc[:],
                                       in_=b0.partition_broadcast(128)), "dma", 16)
                ctx.inc(sync.dma_start(out=dtbbc[:],
                                       in_=dtb.partition_broadcast(128)), "dma", 16)
                ctx.inc(sync.dma_start(out=dbc[:],
                                       in_=dpar.partition_broadcast(128)), "dma", 16)

            @blk.vector
            def _(v):
                ctx.inc(v.memset(ones_bf[:], 1.0), "v")

            for k in range(NK):
                ab = absw if (k % 2 == 0) else absw2

                @blk.scalar
                def _(sc, k=k, ab=ab):
                    ctx.wait(sc, "s")
                    ctx.inc(sc.dma_start(out=xt[:],
                                         in_=w0T[128 * k:128 * (k + 1), :]),
                            "sdma", 16)
                    ctx.wait(sc, "sdma")
                    if k >= 2:
                        ctx.wait(sc, "t", (k - 1) * 4)
                    ctx.inc(sc.activation(out=sw0[:, k * D:(k + 1) * D],
                                          in_=xt[:], func=AF.Sign), "s")
                    ctx.inc(sc.activation(out=ab[:], in_=xt[:], func=AF.Abs),
                            "s")
                    ctx.inc(sc.activation(out=stat[:, 10:11], in_=xt[:, 0:1],
                                          func=AF.Abs), "s")

                @blk.tensor
                def _(t, k=k, ab=ab):
                    ctx.wait(t, "s")
                    ctx.wait(t, "v", 1)
                    for j in range(4):
                        ctx.inc(t.matmul(ps_g[j][0:1, :], ones_bf[:],
                                         ab[:, 512 * j:512 * (j + 1)],
                                         start=(k == 0), stop=(k == NK - 1)),
                                "t")

            @blk.vector
            def _(v):
                ctx.wait(v, "t")
                for j in range(4):
                    ctx.inc(v.tensor_scalar(out=alrow[0:1, 512 * j:512 * (j + 1)],
                                            in0=ps_g[j][0:1, :], scalar1=1.0 / D,
                                            scalar2=None, op0=ALU.mult), "v")

            @blk.sync
            def _(sync):
                ctx.wait(sync, "v")
                ctx.inc(sync.dma_start(out=alpha_d, in_=alrow[0:1, :]),
                        "dma", 16)
                ctx.wait(sync, "dma")
                ctx.inc(sync.dma_start(out=alphabc[:],
                                       in_=alpha_d.partition_broadcast(128)),
                        "dma", 16)
                ctx.wait(sync, "dma")
                ctx.wait(sync, "gdma")

        for i in range(NT):
            with nc.Block() as blk:
                ctx.sem("odma")
                odma_prev = ctx.cnt["odma"]

                @blk.sync
                def _(sync, i=i):
                    ctx.inc(sync.dma_start(out=xt[:],
                                           in_=x[128 * i:128 * (i + 1), :]),
                            "dma", 16)

                @blk.vector
                def _(v):
                    v.wait_ge(ctx.sem("odma"), odma_prev)

                @blk.scalar
                def _(sc):
                    sc.wait_ge(ctx.sem("odma"), odma_prev)

                emit_ln_stats_apply(ctx, blk, xt, xnf, stat, sp1)
                vxn = ctx.cnt["v"]

                @blk.vector
                def _(v):
                    v.wait_ge(ctx.sems["v"], vxn)
                    ctx.inc(v.tensor_copy(out=xnbf_tm[:], in_=xnf[:]), "v")

                vbf = ctx.cnt["v"]

                for k in range(NK):
                    @blk.tensor
                    def _(t, k=k):
                        t.wait_ge(ctx.sems["v"], vbf)
                        if k > 0:
                            ctx.wait(t, "s")
                        ctx.inc(t.transpose(ps_tr[:],
                                            xnbf_tm[:, 128 * k:128 * (k + 1)],
                                            ident_bf[:]), "t")

                    @blk.scalar
                    def _(sc, k=k):
                        ctx.wait(sc, "t")
                        ctx.inc(sc.copy(out=xnTbf[:, 128 * k:128 * (k + 1)],
                                        in_=ps_tr[:]), "s")

                strans = ctx.cnt["s"]

                @blk.tensor
                def _(t, i=i):
                    t.wait_ge(ctx.sems["s"], strans)
                    for k in range(NK):
                        ctx.inc(t.matmul(ps_e[:],
                                         xpTbf[:, k * E_DBL:k * E_DBL + 128],
                                         xnTbf[:, 128 * k:128 * (k + 1)],
                                         start=(k == 0), stop=(k == NK - 1)),
                                "t")
                    for k in range(NK):
                        ctx.inc(t.matmul(ps_f[:],
                                         xpTbf[:, k * E_DBL + 128:(k + 1) * E_DBL],
                                         xnTbf[:, 128 * k:128 * (k + 1)],
                                         start=(k == 0), stop=(k == NK - 1)),
                                "t")

                txdbl = ctx.cnt["t"]

                @blk.vector
                def _(v, i=i):
                    v.wait_ge(ctx.sems["t"], txdbl)
                    ctx.inc(v.tensor_copy(out=xdblTbf[:, 128 * i:128 * (i + 1)],
                                          in_=ps_e[:]), "v")

                @blk.scalar
                def _(sc, i=i):
                    sc.wait_ge(ctx.sems["t"], txdbl)
                    ctx.inc(sc.copy(out=bcT[:, 128 * i:128 * (i + 1)],
                                    in_=ps_f[:]), "s")

                vdt = ctx.cnt["v"]
                sbc = ctx.cnt["s"]

                @blk.tensor
                def _(t, i=i):
                    t.wait_ge(ctx.sems["v"], vdt)
                    t.wait_ge(ctx.sems["s"], sbc)
                    for j in range(4):
                        ctx.inc(t.matmul(ps_g[j][:],
                                         xdblTbf[:, 128 * i:128 * (i + 1)],
                                         dtTbf[:, 512 * j:512 * (j + 1)],
                                         start=True, stop=True), "t")

                tdel = ctx.cnt["t"]

                @blk.vector
                def _(v):
                    v.wait_ge(ctx.sems["t"], tdel)
                    for j in range(4):
                        sl = slice(512 * j, 512 * (j + 1))
                        ctx.inc(v.tensor_tensor(out=zb[:, sl], in0=ps_g[j][:],
                                                in1=dtbbc[:, sl], op=ALU.add),
                                "v")

                vz = ctx.cnt["v"]

                @blk.scalar
                def _(sc):
                    sc.wait_ge(ctx.sems["v"], vz)
                    ctx.inc(sc.activation(out=sp1[:], in_=zb[:], func=AF.Abs),
                            "s")
                    ctx.wait(sc, "s")
                    ctx.inc(sc.activation(out=sp2[:], in_=sp1[:], func=AF.Exp,
                                          scale=-1.0), "s")
                    ctx.wait(sc, "s")
                    ctx.inc(sc.activation(out=sp1[:], in_=sp2[:], func=AF.Ln,
                                          bias=1.0), "s")
                    ctx.wait(sc, "s")
                    ctx.inc(sc.activation(out=sp2[:], in_=zb[:], func=AF.Relu),
                            "s")

                ssp = ctx.cnt["s"]

                @blk.vector
                def _(v):
                    v.wait_ge(ctx.sems["s"], ssp)
                    ctx.inc(v.tensor_tensor(out=deltat[:], in0=sp1[:],
                                            in1=sp2[:], op=ALU.add), "v")
                    ctx.wait(v, "v")
                    ctx.inc(v.tensor_tensor(out=ut[:], in0=deltat[:],
                                            in1=xnf[:], op=ALU.mult), "v")
                    ctx.inc(v.tensor_tensor(out=zb[:], in0=xnf[:], in1=dbc[:],
                                            op=ALU.mult), "v")

                vdu = ctx.cnt["v"]

                @blk.tensor
                def _(t):
                    ctx.wait(t, "v")
                    for j in range(4):
                        for k in range(NK):
                            ctx.inc(t.matmul(ps_g[j][:],
                                             xnTbf[:, 128 * k:128 * (k + 1)],
                                             sw0[:, k * D + 512 * j:
                                                 k * D + 512 * (j + 1)],
                                             start=(k == 0), stop=(k == NK - 1)),
                                    "t")

                tg = ctx.cnt["t"]

                @blk.vector
                def _(v):
                    v.wait_ge(ctx.sems["t"], tg)
                    for j in range(4):
                        sl = slice(512 * j, 512 * (j + 1))
                        ctx.inc(v.tensor_tensor(out=sp1[:, sl], in0=ps_g[j][:],
                                                in1=alphabc[:, sl],
                                                op=ALU.mult), "v")
                        ctx.wait(v, "v")
                        ctx.inc(v.tensor_tensor(out=sp2[:, sl], in0=sp1[:, sl],
                                                in1=b0bc[:, sl], op=ALU.add),
                                "v")

                vge = ctx.cnt["v"]

                @blk.scalar
                def _(sc, i=i):
                    sc.wait_ge(ctx.sems["v"], vge)
                    ctx.inc(sc.activation(out=g0t[:], in_=sp2[:], func=AF.Silu),
                            "s")

                sg0 = ctx.cnt["s"]

                @blk.sync
                def _(sync, i=i):
                    rows = slice(128 * i, 128 * (i + 1))
                    sync.wait_ge(ctx.sems["v"], vdu)
                    ctx.inc(sync.dma_start(out=o_delta[rows, :], in_=deltat[:]),
                            "odma", 16)
                    ctx.inc(sync.dma_start(out=o_u[rows, :], in_=ut[:]),
                            "odma", 16)
                    ctx.inc(sync.dma_start(out=o_dxn[rows, :], in_=zb[:]),
                            "odma", 16)
                    sync.wait_ge(ctx.sems["s"], sg0)
                    ctx.inc(sync.dma_start(out=o_g0[rows, :], in_=g0t[:]),
                            "odma", 16)

        with nc.Block() as blk:
            @blk.sync
            def _(sync):
                ctx.inc(sync.dma_start(out=o_bc, in_=bcT[:]), "odma", 16)
                ctx.wait(sync, "odma")
    return nc



_NC_A = None


def _get_nc_A():
    global _NC_A
    if _NC_A is None:
        _NC_A = build_launch_A()
    return _NC_A


def run_A(inputs):
    x_full = np.ascontiguousarray(inputs["L0_attn_x"], dtype=np.float32)
    xpT = np.ascontiguousarray(inputs["L0_attn_x_proj_w"].T)
    dtT = np.ascontiguousarray(inputs["L0_attn_dt_proj_w"].T)
    dtb = np.ascontiguousarray(inputs["L0_attn_dt_proj_b"].reshape(1, D))
    w0T = np.ascontiguousarray(inputs["L0_ffn_weight"].T)
    b0 = np.ascontiguousarray(inputs["L0_ffn_bias"].reshape(1, D))
    dpar = np.ascontiguousarray(inputs["L0_attn_D_param"].reshape(1, D))

    in_maps = []
    for c in range(NCORES):
        b, q = divmod(c, 4)
        rows = x_full[b, RPC * q:RPC * (q + 1), :]
        in_maps.append(dict(x=np.ascontiguousarray(rows), xpT=xpT, dtT=dtT,
                            dtb=dtb, w0T=w0T, b0=b0, dpar=dpar))
    res = run_bass_kernel_spmd(_get_nc_A(), in_maps, core_ids=list(range(NCORES)))

    def gather(name):
        out = np.empty((B, L, D), np.float32)
        for c in range(NCORES):
            b, q = divmod(c, 4)
            out[b, RPC * q:RPC * (q + 1), :] = res.results[c][name]
        return out

    delta = gather("o_delta")
    u = gather("o_u")
    g0 = gather("o_g0")
    dxn = gather("o_dxn")
    bc = np.empty((B, 2 * N_STATE, L), np.float32)
    for c in range(NCORES):
        b, q = divmod(c, 4)
        bc[b, :, RPC * q:RPC * (q + 1)] = res.results[c]["o_bc"]
    return delta, u, g0, dxn, bc[:, :N_STATE, :], bc[:, N_STATE:, :]





def build_launch_B():
    nc = bass.Bass("TRN2", target_bir_lowering=False, debug=False,
                   num_devices=NCORES)
    ctx = Ctx(nc)
    T = 512
    NCH = L // T
    KT = DSL // 128

    dT = nc.dram_tensor("dT", [DSL, L], F32, kind="ExternalInput").ap()
    uT = nc.dram_tensor("uT", [DSL, L], F32, kind="ExternalInput").ap()
    bT = nc.dram_tensor("bT", [N_STATE, L], F32, kind="ExternalInput").ap()
    cT = nc.dram_tensor("cT", [N_STATE, L], F32, kind="ExternalInput").ap()
    alog = nc.dram_tensor("alog", [DSL, N_STATE], F32, kind="ExternalInput").ap()
    o_y = nc.dram_tensor("o_y", [DSL, L], F32, kind="ExternalOutput").ap()

    dTr = dT.rearrange("(k p) t -> p k t", p=128)
    uTr = uT.rearrange("(k p) t -> p k t", p=128)
    oyr = o_y.rearrange("(k p) t -> p k t", p=128)
    alr = alog.rearrange("(k p) n -> p k n", p=128)

    at4 = ctx.sb("at4", [128, KT * N_STATE], F32)
    carry = ctx.sb("carry", [128, KT * N_STATE], F32)
    dbuf = [ctx.sb(f"dbuf{i}", [128, KT * T], F32) for i in range(2)]
    ubuf = [ctx.sb(f"ubuf{i}", [128, KT * T], F32) for i in range(2)]
    bbc = [ctx.sb(f"bbc{i}", [128, N_STATE * T], F32) for i in range(2)]
    cbc = [ctx.sb(f"cbc{i}", [128, N_STATE * T], F32) for i in range(2)]
    abuf = [ctx.sb(f"abuf{i}", [128, T], F32) for i in range(4)]
    bmul = ctx.sb("bmul", [128, T], F32)
    hbuf = [ctx.sb(f"hbuf{i}", [128, T], F32) for i in range(2)]
    hc = [ctx.sb(f"hc{i}", [128, T], F32) for i in range(2)]
    yg = [ctx.sb(f"yg{i}", [128, T], F32) for i in range(2)]

    with ctx.es:
        with nc.Block() as blk:
            @blk.sync
            def _(sync):
                for kt in range(KT):
                    ctx.inc(sync.dma_start(
                        out=at4[:, kt * N_STATE:(kt + 1) * N_STATE],
                        in_=alr[:, kt, :]), "dma", 16)

            @blk.vector
            def _(v):
                ctx.inc(v.memset(carry[:], 0.0), "v")

            @blk.scalar
            def _(sc):
                ctx.wait(sc, "dma")
                ctx.inc(sc.activation(out=at4[:], in_=at4[:], func=AF.Exp), "s")

            @blk.vector
            def _(v):
                ctx.wait(v, "s")
                ctx.inc(v.tensor_scalar(out=at4[:], in0=at4[:], scalar1=-1.0,
                                        scalar2=None, op0=ALU.mult), "v")

        sem = ctx.sem
        sem("odma")
        sem("gp")
        it = 0
        v_tts = {}
        gp_cnt_at = {}
        vend = {}
        ybuf_dma = {0: 0, 1: 0}
        with nc.Block() as blk:
            for ch in range(NCH):
                pb = ch % 2
                sl = slice(T * ch, T * (ch + 1))

                @blk.sync
                def _(sync, ch=ch, pb=pb, sl=sl):
                    if ch >= 2:
                        sync.wait_ge(ctx.sems["v"], vend[ch - 2])
                    ctx.inc(sync.dma_start(out=dbuf[pb][:],
                                           in_=dTr[:, :, sl]), "dma", 16)
                    ctx.inc(sync.dma_start(out=ubuf[pb][:],
                                           in_=uTr[:, :, sl]), "dma", 16)
                    ctx.inc(sync.dma_start(
                        out=bbc[pb][:].rearrange("p (n t) -> p n t", n=N_STATE),
                        in_=bT[:, sl].unsqueeze(0).partition_broadcast(128)),
                        "dma", 16)
                    ctx.inc(sync.dma_start(
                        out=cbc[pb][:].rearrange("p (n t) -> p n t", n=N_STATE),
                        in_=cT[:, sl].unsqueeze(0).partition_broadcast(128)),
                        "dma", 16)

                dma_done = ctx.cnt["dma"]

                for kt in range(KT):
                    for n in range(N_STATE):
                        ab = abuf[it % 4]
                        hb = hbuf[it % 2]
                        hcb = hc[it % 2]
                        col = kt * N_STATE + n

                        @blk.scalar
                        def _(sc, ab=ab, kt=kt, n=n, it=it, pb=pb,
                              dma_done=dma_done):
                            sc.wait_ge(ctx.sems["dma"], dma_done)
                            if it >= 4 and (it - 4) in v_tts:
                                sc.wait_ge(ctx.sems["v"], v_tts[it - 4])
                            ctx.inc(sc.activation(
                                out=ab[:],
                                in_=dbuf[pb][:, kt * T:(kt + 1) * T],
                                func=AF.Exp,
                                scale=at4[:, col:col + 1]), "s")

                        s_a = ctx.cnt["s"]

                        @blk.vector
                        def _(v, ab=ab, hb=hb, hcb=hcb, kt=kt, n=n, it=it,
                              pb=pb, col=col, s_a=s_a, dma_done=dma_done):
                            v.wait_ge(ctx.sems["dma"], dma_done)
                            v.tensor_tensor(
                                out=bmul[:],
                                in0=ubuf[pb][:, kt * T:(kt + 1) * T],
                                in1=bbc[pb][:, n * T:(n + 1) * T],
                                op=ALU.mult)
                            v.wait_ge(ctx.sems["s"], s_a)
                            ctx.inc(v.tensor_tensor_scan(
                                out=hb[:], data0=ab[:], data1=bmul[:],
                                initial=carry[:, col:col + 1],
                                op0=ALU.mult, op1=ALU.add), "v")
                            v_tts[it] = ctx.cnt["v"]
                            ctx.inc(v.tensor_tensor(
                                out=hcb[:], in0=hb[:],
                                in1=cbc[pb][:, n * T:(n + 1) * T],
                                op=ALU.mult), "v")
                            v.tensor_copy(out=carry[:, col:col + 1],
                                          in_=hb[:, T - 1:T])

                        v_hc = ctx.cnt["v"]

                        @blk.vector
                        def _(v, hcb=hcb, n=n, kt=kt, it=it):
                            if n == 0:
                                v.wait_ge(ctx.sems["odma"], ybuf_dma[kt % 2])
                                ctx.inc(v.tensor_copy(out=yg[kt % 2][:],
                                                      in_=hcb[:]), "v")
                            else:
                                ctx.inc(v.tensor_tensor(
                                    out=yg[kt % 2][:], in0=yg[kt % 2][:],
                                    in1=hcb[:], op=ALU.add), "v")
                            gp_cnt_at[it] = ctx.cnt["v"]

                        it += 1

                    @blk.sync
                    def _(sync, kt=kt, sl=sl):
                        sync.wait_ge(ctx.sems["v"], ctx.cnt["v"])
                        ctx.inc(sync.dma_start(out=oyr[:, kt, sl],
                                               in_=yg[kt % 2][:]), "odma", 16)

                    ybuf_dma[kt % 2] = ctx.cnt["odma"]

                vend[ch] = ctx.cnt["v"]

        with nc.Block() as blk:
            @blk.sync
            def _(sync):
                ctx.wait(sync, "odma")
    return nc


_NC_B = None


def _get_nc_B():
    global _NC_B
    if _NC_B is None:
        _NC_B = build_launch_B()
    return _NC_B


def run_B(inputs, delta, u, bT, cT):
    deltaT = np.ascontiguousarray(delta.transpose(0, 2, 1))
    uT = np.ascontiguousarray(u.transpose(0, 2, 1))
    alog = np.ascontiguousarray(inputs["L0_attn_A_log"], dtype=np.float32)
    in_maps = []
    for c in range(NCORES):
        b, q = divmod(c, 4)
        ds = slice(DSL * q, DSL * (q + 1))
        in_maps.append(dict(
            dT=np.ascontiguousarray(deltaT[b, ds, :]),
            uT=np.ascontiguousarray(uT[b, ds, :]),
            bT=np.ascontiguousarray(bT[b]),
            cT=np.ascontiguousarray(cT[b]),
            alog=np.ascontiguousarray(alog[ds, :])))
    res = run_bass_kernel_spmd(_get_nc_B(), in_maps, core_ids=list(range(NCORES)))
    yT = np.empty((B, D, L), np.float32)
    for c in range(NCORES):
        b, q = divmod(c, 4)
        yT[b, DSL * q:DSL * (q + 1), :] = res.results[c]["o_y"]
    return np.ascontiguousarray(yT.transpose(0, 2, 1))



def build_launch_AB():
    nc = bass.Bass("TRN2", target_bir_lowering=False, debug=False,
                   num_devices=NCORES)
    ctx = Ctx(nc)
    T = 512
    NCH = L // T
    KT = DSL // 128

    xT = nc.dram_tensor("xT", [D, L], F32, kind="ExternalInput").ap()
    xpT = nc.dram_tensor("xpT", [D, E_DBL], F32, kind="ExternalInput").ap()
    dtTs = nc.dram_tensor("dtTs", [DT_RANK, DSL], F32, kind="ExternalInput").ap()
    dtbs = nc.dram_tensor("dtbs", [128, KT], F32, kind="ExternalInput").ap()
    w0Ts = nc.dram_tensor("w0Ts", [D, DSL], F32, kind="ExternalInput").ap()
    b0s = nc.dram_tensor("b0s", [128, KT], F32, kind="ExternalInput").ap()
    dpars = nc.dram_tensor("dpars", [128, KT], F32, kind="ExternalInput").ap()
    alog = nc.dram_tensor("alog", [DSL, N_STATE], F32, kind="ExternalInput").ap()

    o_y = nc.dram_tensor("o_y", [DSL, L], F32, kind="ExternalOutput").ap()
    o_g0T = nc.dram_tensor("o_g0T", [DSL, L], F32, kind="ExternalOutput").ap()
    o_dxnT = nc.dram_tensor("o_dxnT", [DSL, L], F32, kind="ExternalOutput").ap()

    dT_d = nc.dram_tensor("dT_d", [DSL, L], F32, kind="Internal").ap()
    uT_d = nc.dram_tensor("uT_d", [DSL, L], F32, kind="Internal").ap()
    bc_d = nc.dram_tensor("bc_d", [2 * N_STATE, L], BF16, kind="Internal").ap()
    ms_d = nc.dram_tensor("ms_d", [2, L], F32, kind="Internal").ap()

    dTr = dT_d.rearrange("(k p) t -> p k t", p=128)
    uTr = uT_d.rearrange("(k p) t -> p k t", p=128)
    oyr = o_y.rearrange("(k p) t -> p k t", p=128)
    og0r = o_g0T.rearrange("(k p) t -> p k t", p=128)
    odxr = o_dxnT.rearrange("(k p) t -> p k t", p=128)
    alr = alog.rearrange("(k p) n -> p k n", p=128)
    xTr = xT.rearrange("(k p) t -> p k t", p=128)

    consts = ctx.sb("consts", [128, 4], F32)
    ones_bf = ctx.sb("onesb", [128, 1], BF16)
    xpTbf = ctx.sb("xpTbf", [128, NK * E_DBL], BF16)
    dtTbf = ctx.sb("dtTbf", [128, DSL], BF16)
    sw0 = ctx.sb("sw0", [128, NK * DSL], BF16)
    alpha4 = ctx.sb("alpha4", [128, KT], F32)
    w0rsn = ctx.sb("w0rsn", [128, KT], F32)
    xpsn = ctx.sb("xpsn", [128, 2], F32)
    dtb4 = ctx.sb("dtb4", [128, KT], F32)
    b04 = ctx.sb("b04", [128, KT], F32)
    dp4 = ctx.sb("dp4", [128, KT], F32)
    at4 = ctx.sb("at4", [128, KT * N_STATE], F32)
    carry = ctx.sb("carry", [128, KT * N_STATE], F32)
    xbf = [ctx.sb(f"xbf{i}", [128, T], BF16) for i in range(2)]
    x2bf = ctx.sb("x2bf", [128, T], BF16)
    musd = ctx.sb("musd", [1, 4 * T], F32)
    mubc = ctx.sb("mubc", [128, T], F32)
    rsbc = ctx.sb("rsbc", [128, T], F32)
    xdbl0 = ctx.sb("xdbl0", [128, T], F32)
    xdbl1 = ctx.sb("xdbl1", [32, T], F32)
    dtbf = ctx.sb("dtbf", [128, T], BF16)
    xfs = ctx.sb("xfs", [128, KT * T], F32)
    xnsl = ctx.sb("xnsl", [128, KT * T], F32)
    gtmp = ctx.sb("gtmp", [128, KT * T], F32)
    dl1 = [ctx.sb(f"dl1_{i}", [128, KT * T], F32) for i in range(2)]
    sp1 = ctx.sb("sp1ab", [128, KT * T], F32)
    sp2 = [ctx.sb(f"sp2ab{i}", [128, KT * T], F32) for i in range(2)]
    _pad = ctx.sb("padab", [128, 2 * KT * T - 256], F32)
    bbc = [ctx.sb(f"bbc{i}", [128, N_STATE * T], BF16) for i in range(2)]
    cbc = [ctx.sb(f"cbc{i}", [128, N_STATE * T], BF16) for i in range(2)]
    abuf = [ctx.sb(f"abuf{i}", [128, T], F32) for i in range(3)]
    bmul = ctx.sb("bmul", [128, T], F32)
    hbuf = [ctx.sb(f"hbuf{i}", [128, T], F32) for i in range(2)]
    hc = [ctx.sb(f"hc{i}", [128, T], F32) for i in range(2)]
    yg = [ctx.sb(f"yg{i}", [128, T], F32) for i in range(2)]
    bcbf = ctx.sb("bcbf", [32, T], BF16)

    ps_x0 = ctx.ps("ps_x0", [128, T], F32)
    ps_x1 = ctx.ps("ps_x1", [32, T], F32)
    ps_s1 = ctx.ps("ps_s1", [1, T], F32)
    ps_s2 = ctx.ps("ps_s2", [1, T], F32)
    ps_g = [ctx.ps(f"ps_gg{j}", [128, T], F32) for j in range(4)]

    with ctx.es:
        for j, val in enumerate([EPS, -1.0, 1.0, 0.0]):
            nc.const_aps.register(F32, val, consts[:, j:j + 1])

        with nc.Block() as blk:
            @blk.gpsimd
            def _(gp):
                for j, val in enumerate([EPS, -1.0, 1.0, 0.0]):
                    gp.memset(consts[:, j:j + 1], val)
                for k in range(NK):
                    ctx.inc(gp.dma_start(out=xpTbf[:, k * E_DBL:(k + 1) * E_DBL],
                                         in_=xpT[128 * k:128 * (k + 1), :]),
                            "gdma", 16)
                ctx.inc(gp.dma_start(out=dtTbf[:], in_=dtTs), "gdma", 16)

            @blk.sync
            def _(sync):
                ctx.inc(sync.dma_start(out=dtb4[:], in_=dtbs), "dma", 16)
                ctx.inc(sync.dma_start(out=b04[:], in_=b0s), "dma", 16)
                ctx.inc(sync.dma_start(out=dp4[:], in_=dpars), "dma", 16)
                for kt in range(KT):
                    ctx.inc(sync.dma_start(
                        out=at4[:, kt * N_STATE:(kt + 1) * N_STATE],
                        in_=alr[:, kt, :]), "dma", 16)

            @blk.vector
            def _(v):
                ctx.inc(v.memset(ones_bf[:], 1.0), "v")
                ctx.inc(v.memset(carry[:], 0.0), "v")

            @blk.scalar
            def _(sc):
                ctx.wait(sc, "dma")
                ctx.inc(sc.activation(out=at4[:], in_=at4[:], func=AF.Exp), "s")

            @blk.vector
            def _(v):
                ctx.wait(v, "s")
                ctx.inc(v.tensor_scalar(out=at4[:], in0=at4[:], scalar1=-1.0,
                                        scalar2=None, op0=ALU.mult), "v")

            prep_s = {}
            for k in range(NK):
                wcol = (k % 2) * DSL

                @blk.sync
                def _(sync, k=k, wcol=wcol):
                    if k >= 2:
                        sync.wait_ge(ctx.sems["s"], prep_s[k - 2])
                    ctx.inc(sync.dma_start(out=xfs[:, wcol:wcol + DSL],
                                           in_=w0Ts[128 * k:128 * (k + 1), :]),
                            "dma", 16)

                pld = ctx.cnt["dma"]

                @blk.scalar
                def _(sc, k=k, wcol=wcol, pld=pld):
                    sc.wait_ge(ctx.sems["dma"], pld)
                    if k >= 1:
                        ctx.wait(sc, "t")
                    ctx.inc(sc.activation(out=sw0[:, DSL * k:DSL * (k + 1)],
                                          in_=xfs[:, wcol:wcol + DSL],
                                          func=AF.Sign), "s")
                    ctx.inc(sc.activation(
                        out=sp1.bitcast(BF16)[:, wcol:wcol + DSL],
                        in_=xfs[:, wcol:wcol + DSL], func=AF.Abs), "s")
                    ctx.inc(sc.activation(out=x2bf[:, 0:1],
                                          in_=xfs[:, wcol:wcol + 1],
                                          func=AF.Abs), "s")

                prep_s[k] = ctx.cnt["s"]

                @blk.tensor
                def _(t, k=k, wcol=wcol):
                    ctx.wait(t, "s")
                    if k == 0:
                        ctx.wait(t, "v")
                    for kt in range(KT):
                        ctx.inc(t.matmul(
                            ps_g[kt][:, 0:1],
                            sp1.bitcast(BF16)[:, wcol + 128 * kt:
                                              wcol + 128 * (kt + 1)],
                            ones_bf[:],
                            start=(k == 0), stop=(k == NK - 1)), "t")

            ta = ctx.cnt["t"]

            @blk.vector
            def _(v):
                v.wait_ge(ctx.sems["t"], ta)
                for kt in range(KT):
                    ctx.inc(v.tensor_scalar(out=alpha4[:, kt:kt + 1],
                                            in0=ps_g[kt][:, 0:1],
                                            scalar1=1.0 / D, scalar2=None,
                                            op0=ALU.mult), "v")

            va = ctx.cnt["v"]

            @blk.tensor
            def _(t):
                t.wait_ge(ctx.sems["v"], va)
                for kt in range(KT):
                    for k in range(NK):
                        ctx.inc(t.matmul(ps_g[kt][:, 0:1],
                                         sw0[:, DSL * k + 128 * kt:
                                             DSL * k + 128 * (kt + 1)],
                                         ones_bf[:],
                                         start=(k == 0), stop=(k == NK - 1)),
                                "t")
                for k in range(NK):
                    ctx.inc(t.matmul(ps_s1[0:1, 0:1] if False else
                                     ps_x0[:, 0:1],
                                     xpTbf[:, k * E_DBL:k * E_DBL + 128],
                                     ones_bf[:],
                                     start=(k == 0), stop=(k == NK - 1)), "t")
                for k in range(NK):
                    ctx.inc(t.matmul(ps_x1[:, 0:1],
                                     xpTbf[:, k * E_DBL + 128:(k + 1) * E_DBL],
                                     ones_bf[:],
                                     start=(k == 0), stop=(k == NK - 1)), "t")

            tb = ctx.cnt["t"]

            @blk.vector
            def _(v):
                v.wait_ge(ctx.sems["t"], tb)
                for kt in range(KT):
                    ctx.inc(v.tensor_scalar(out=w0rsn[:, kt:kt + 1],
                                            in0=ps_g[kt][:, 0:1],
                                            scalar1=-1.0, scalar2=None,
                                            op0=ALU.mult), "v")
                ctx.inc(v.tensor_scalar(out=xpsn[:, 0:1], in0=ps_x0[:, 0:1],
                                        scalar1=-1.0, scalar2=None,
                                        op0=ALU.mult), "v")
                ctx.inc(v.tensor_scalar(out=xpsn[0:32, 1:2], in0=ps_x1[:, 0:1],
                                        scalar1=-1.0, scalar2=None,
                                        op0=ALU.mult), "v")

        sem = ctx.sem
        sem("odma")
        sem("vsc")
        sem("gpc")
        sem("pc")
        awr = {}
        it = 0
        vend = {}
        schunk_end = {}
        loads_done = {}
        ybuf_dma = {0: 0, 1: 0}

        def emit_A(blk, ch):
            sl = slice(T * ch, T * (ch + 1))
            pb = ch % 2

            for k in range(NK):
                xb = xbf[k % 2]

                @blk.gpsimd
                def _(gp, k=k, ch=ch, xb=xb):
                    key = ("xk", k % 2)
                    if key in awr:
                        gp.wait_ge(ctx.sems["t"], awr[key])
                    ctx.inc(gp.dma_start(out=xb[:],
                                         in_=xTr[:, k, sl]), "gdma", 16)

                gld = ctx.cnt["gdma"]

                @blk.gpsimd
                def _(gp, k=k, gld=gld, xb=xb):
                    gp.wait_ge(ctx.sems["gdma"], gld)
                    if ("x2", 0) in awr:
                        gp.wait_ge(ctx.sems["t"], awr[("x2", 0)])
                    ctx.inc(gp.tensor_tensor(out=x2bf[:], in0=xb[:],
                                             in1=xb[:], op=ALU.mult), "gpc")

                ssq = ctx.cnt["gpc"]

                @blk.tensor
                def _(t, k=k, ch=ch, gld=gld, ssq=ssq, xb=xb):
                    t.wait_ge(ctx.sems["gdma"], gld)
                    t.wait_ge(ctx.sems["gpc"], ssq)
                    if k == 0 and ch > 0:
                        t.wait_ge(ctx.sems["pc"], ch)
                    ctx.inc(t.matmul(ps_s1[0:1, :], ones_bf[:], xb[:],
                                     start=(k == 0), stop=(k == NK - 1)), "t")
                    ctx.inc(t.matmul(ps_s2[0:1, :], ones_bf[:], x2bf[:],
                                     start=(k == 0), stop=(k == NK - 1)), "t")
                    awr[("x2", 0)] = ctx.cnt["t"]
                    ctx.inc(t.matmul(ps_x0[:],
                                     xpTbf[:, k * E_DBL:k * E_DBL + 128],
                                     xb[:], start=(k == 0),
                                     stop=(k == NK - 1)), "t")
                    ctx.inc(t.matmul(ps_x1[:],
                                     xpTbf[:, k * E_DBL + 128:(k + 1) * E_DBL],
                                     xb[:], start=(k == 0),
                                     stop=(k == NK - 1)), "t")
                    for kt in range(KT):
                        ctx.inc(t.matmul(ps_g[kt][:],
                                         sw0[:, DSL * k + 128 * kt:
                                             DSL * k + 128 * (kt + 1)],
                                         xb[:], start=(k == 0),
                                         stop=(k == NK - 1)), "t")
                    awr[("xk", k % 2)] = ctx.cnt["t"]

            awr[("tmm", ch)] = ctx.cnt["t"]

        post_dmad = {}
        post_ssp = {}
        post_vdu = {}
        post_vbc = {}
        s_of = {}

        def emit_A_post_p1(blk, ch):
            sl = slice(T * ch, T * (ch + 1))
            tmm = awr[("tmm", ch)]

            @blk.vector
            def _(v, tmm=tmm):
                v.wait_ge(ctx.sems["t"], tmm)
                v.tensor_scalar(out=musd[0:1, 0:T], in0=ps_s1[0:1, :],
                                scalar1=1.0 / D, scalar2=None, op0=ALU.mult)
                ctx.inc(v.tensor_scalar(out=musd[0:1, T:2 * T],
                                        in0=ps_s2[0:1, :], scalar1=1.0 / D,
                                        scalar2=None, op0=ALU.mult), "v")
                ctx.wait(v, "v")
                ctx.inc(v.tensor_tensor(out=musd[0:1, 2 * T:3 * T],
                                        in0=musd[0:1, 0:T],
                                        in1=musd[0:1, 0:T], op=ALU.mult), "v")
                ctx.wait(v, "v")
                ctx.inc(v.tensor_tensor(out=musd[0:1, 2 * T:3 * T],
                                        in0=musd[0:1, T:2 * T],
                                        in1=musd[0:1, 2 * T:3 * T],
                                        op=ALU.subtract), "v")

            vst = ctx.cnt["v"]

            @blk.scalar
            def _(sc, vst=vst):
                sc.wait_ge(ctx.sems["v"], vst)
                ctx.inc(sc.activation(out=musd[0:1, 3 * T:4 * T],
                                      in_=musd[0:1, 2 * T:3 * T],
                                      func=AF.Sqrt, bias=EPS), "s")

            ssd = ctx.cnt["s"]

            @blk.vector
            def _(v, ssd=ssd):
                v.wait_ge(ctx.sems["s"], ssd)
                ctx.inc(v.reciprocal(out=musd[0:1, 2 * T:3 * T],
                                     in_=musd[0:1, 3 * T:4 * T]), "v")
                ctx.wait(v, "v")

            vrs = ctx.cnt["v"]

            @blk.sync
            def _(sync, ch=ch, sl=sl, vrs=vrs):
                sync.wait_ge(ctx.sems["v"], vrs)
                ctx.inc(sync.dma_start(out=ms_d[0:1, sl],
                                       in_=musd[0:1, 0:T]), "dma", 16)
                ctx.inc(sync.dma_start(out=ms_d[1:2, sl],
                                       in_=musd[0:1, 2 * T:3 * T]), "dma", 16)
                ctx.wait(sync, "dma")
                ctx.inc(sync.dma_start(out=mubc[:],
                                       in_=ms_d[0:1, sl].partition_broadcast(128)),
                        "dma", 16)
                ctx.inc(sync.dma_start(out=rsbc[:],
                                       in_=ms_d[1:2, sl].partition_broadcast(128)),
                        "dma", 16)
                ctx.inc(sync.dma_start(
                    out=xfs[:].rearrange("p (k t) -> p k t", k=KT),
                    in_=xTr[:, 0:KT, sl]), "dma", 16)

            post_dmad[ch] = ctx.cnt["dma"]

        def emit_A_post_p2(blk, ch):
            sl = slice(T * ch, T * (ch + 1))
            pb = ch % 2
            tmm = awr[("tmm", ch)]
            dmad = post_dmad[ch]

            @blk.vector
            def _(v, dmad=dmad, tmm=tmm, ch=ch):
                v.wait_ge(ctx.sems["dma"], dmad)
                if ("sg0" in awr):
                    v.wait_ge(ctx.sems["s"], awr["sg0"])
                v.scalar_tensor_tensor(out=xdbl0[:], in0=mubc[:],
                                       scalar=xpsn[:, 0:1], in1=ps_x0[:],
                                       op0=ALU.mult, op1=ALU.add)
                ctx.inc(v.tensor_tensor(out=xdbl0[:], in0=xdbl0[:],
                                        in1=rsbc[:], op=ALU.mult), "v")
                v.scalar_tensor_tensor(out=xdbl1[:], in0=mubc[0:32, :],
                                       scalar=xpsn[0:32, 1:2], in1=ps_x1[:],
                                       op0=ALU.mult, op1=ALU.add)
                ctx.inc(v.tensor_tensor(out=xdbl1[:], in0=xdbl1[:],
                                        in1=rsbc[0:32, :], op=ALU.mult), "v")
                ctx.wait(v, "v")
                ctx.inc(v.tensor_copy(out=dtbf[:], in_=xdbl0[:]), "v")
                ctx.inc(v.tensor_copy(out=bcbf[:], in_=xdbl1[:]), "v")
                for kt in range(KT):
                    v.scalar_tensor_tensor(out=gtmp[:, T * kt:T * (kt + 1)],
                                           in0=mubc[:],
                                           scalar=w0rsn[:, kt:kt + 1],
                                           in1=ps_g[kt][:],
                                           op0=ALU.mult, op1=ALU.add)
                    ctx.inc(v.tensor_tensor(out=gtmp[:, T * kt:T * (kt + 1)],
                                            in0=gtmp[:, T * kt:T * (kt + 1)],
                                            in1=rsbc[:], op=ALU.mult), "v")
                    ctx.wait(v, "v")
                    ctx.inc(v.tensor_scalar(out=sp1[:, T * kt:T * (kt + 1)],
                                            in0=gtmp[:, T * kt:T * (kt + 1)],
                                            scalar1=alpha4[:, kt:kt + 1],
                                            scalar2=b04[:, kt:kt + 1],
                                            op0=ALU.mult, op1=ALU.add), "v")
                for kt in range(KT):
                    v.tensor_tensor(out=xnsl[:, T * kt:T * (kt + 1)],
                                    in0=xfs[:, T * kt:T * (kt + 1)],
                                    in1=mubc[:], op=ALU.subtract)
                    ctx.inc(v.tensor_tensor(out=xnsl[:, T * kt:T * (kt + 1)],
                                            in0=xnsl[:, T * kt:T * (kt + 1)],
                                            in1=rsbc[:], op=ALU.mult), "v")

            vcor = ctx.cnt["v"]
            post_vbc[ch] = vcor

            @blk.tensor
            def _(t, vcor=vcor):
                t.wait_ge(ctx.sems["v"], vcor)
                for kt in range(KT):
                    ctx.inc(t.matmul(ps_g[kt][:],
                                     dtTbf[:, 128 * kt:128 * (kt + 1)],
                                     dtbf[:], start=True, stop=True), "t")

            tdel = ctx.cnt["t"]

            @blk.vector
            def _(v, tdel=tdel, ch=ch, pb=pb):
                v.wait_ge(ctx.sems["t"], tdel)
                if ch >= 2:
                    v.wait_ge(ctx.sems["s"], schunk_end[ch - 2])
                for kt in range(KT):
                    ctx.inc(v.tensor_scalar(out=dl1[pb][:, T * kt:T * (kt + 1)],
                                            in0=ps_g[kt][:],
                                            scalar1=1.0,
                                            scalar2=dtb4[:, kt:kt + 1],
                                            op0=ALU.mult, op1=ALU.add), "v")
                ctx.inc(v.tensor_copy(out=musd[0:1, 0:1],
                                      in_=musd[0:1, 1:2]), "pc")

            vz = ctx.cnt["v"]

            @blk.scalar
            def _(sc, vz=vz, ch=ch, pb=pb):
                sc.wait_ge(ctx.sems["v"], vz)
                if ch >= 2:
                    sc.wait_ge(ctx.sems["v"], vend[ch - 2])
                ctx.inc(sc.activation(out=sp2[pb][:], in_=dl1[pb][:],
                                      func=AF.Abs), "s")
                ctx.wait(sc, "s")
                ctx.inc(sc.activation(out=sp2[pb][:], in_=sp2[pb][:],
                                      func=AF.Exp, scale=-1.0), "s")
                ctx.wait(sc, "s")
                ctx.inc(sc.activation(out=sp2[pb][:], in_=sp2[pb][:],
                                      func=AF.Ln, bias=1.0), "s")
                ctx.wait(sc, "s")
                ctx.inc(sc.activation(out=dl1[pb][:], in_=dl1[pb][:],
                                      func=AF.Relu), "s")

            post_ssp[ch] = ctx.cnt["s"]

            @blk.sync
            def _(sync, ch=ch, sl=sl, pb=pb, vcor=vcor):
                sync.wait_ge(ctx.sems["v"], vcor)
                ctx.inc(sync.dma_start(out=bc_d[:, sl], in_=bcbf[:]),
                        "odma", 16)
                ctx.wait(sync, "odma")
                if ch >= 2:
                    sync.wait_ge(ctx.sems["v"], vend[ch - 2])
                ctx.inc(sync.dma_start(
                    out=bbc[pb][:].rearrange("p (n t) -> p n t", n=N_STATE),
                    in_=bc_d[0:N_STATE, sl].unsqueeze(0)
                    .partition_broadcast(128)), "dma", 16)
                ctx.inc(sync.dma_start(
                    out=cbc[pb][:].rearrange("p (n t) -> p n t", n=N_STATE),
                    in_=bc_d[N_STATE:2 * N_STATE, sl].unsqueeze(0)
                    .partition_broadcast(128)), "dma", 16)

            loads_done[ch] = ctx.cnt["dma"]

        def emit_A_post_p3(blk, ch):
            sl = slice(T * ch, T * (ch + 1))
            pb = ch % 2
            ssp = post_ssp[ch]

            @blk.vector
            def _(v, ssp=ssp, ch=ch, pb=pb):
                v.wait_ge(ctx.sems["s"], ssp)
                ctx.inc(v.tensor_tensor(out=dl1[pb][:], in0=dl1[pb][:],
                                        in1=sp2[pb][:], op=ALU.add), "v")
                ctx.wait(v, "v")
                ctx.inc(v.tensor_tensor(out=sp2[pb][:], in0=dl1[pb][:],
                                        in1=xnsl[:], op=ALU.mult), "v")
                for kt in range(KT):
                    ctx.inc(v.tensor_scalar(out=xnsl[:, T * kt:T * (kt + 1)],
                                            in0=xnsl[:, T * kt:T * (kt + 1)],
                                            scalar1=dp4[:, kt:kt + 1],
                                            scalar2=None, op0=ALU.mult), "v")

            vdu = ctx.cnt["v"]
            post_vdu[ch] = vdu

            @blk.scalar
            def _(sc, vdu=vdu):
                sc.wait_ge(ctx.sems["v"], vdu)
                ctx.inc(sc.activation(out=gtmp[:], in_=sp1[:], func=AF.Silu),
                        "s")

            sg0 = ctx.cnt["s"]
            awr["sg0"] = sg0

            @blk.sync
            def _(sync, ch=ch, sl=sl, pb=pb, vdu=vdu, sg0=sg0):
                sync.wait_ge(ctx.sems["v"], vdu)
                ctx.inc(sync.dma_start(
                    out=odxr[:, :, sl],
                    in_=xnsl[:].rearrange("p (k t) -> p k t", k=KT)),
                    "odma", 16)
                sync.wait_ge(ctx.sems["s"], sg0)
                ctx.inc(sync.dma_start(
                    out=og0r[:, :, sl],
                    in_=gtmp[:].rearrange("p (k t) -> p k t", k=KT)),
                    "odma", 16)

        def emit_scan_part(blk, ch, j0, j1):
            pb = ch % 2
            sl = slice(T * ch, T * (ch + 1))
            dma_done = loads_done[ch]
            it0 = ch * KT * N_STATE

            vdu_ch = post_vdu[ch]

            @blk.scalar
            def _(sc, pb=pb, it0=it0, j0=j0, j1=j1, vdu_ch=vdu_ch):
                sc.wait_ge(ctx.sems["v"], vdu_ch)
                for j in range(j0, j1):
                    kt, n = divmod(j, N_STATE)
                    itg = it0 + j
                    col = kt * N_STATE + n
                    if itg >= 3:
                        sc.wait_ge(ctx.sems["vsc"], itg - 2)
                    ctx.inc(sc.activation(
                        out=abuf[itg % 3][:],
                        in_=dl1[pb][:, kt * T:(kt + 1) * T],
                        func=AF.Exp, scale=at4[:, col:col + 1]), "s")
                    s_of[itg] = ctx.cnt["s"]

            if j1 == KT * N_STATE:
                schunk_end[ch] = ctx.cnt["s"]

            for j in range(j0, j1):
                kt, n = divmod(j, N_STATE)
                itg = it0 + j
                ab = abuf[itg % 3]
                hb = hbuf[itg % 2]
                hcb = hc[itg % 2]
                col = kt * N_STATE + n

                @blk.vector
                def _(v, ab=ab, hb=hb, hcb=hcb, kt=kt, n=n, it=itg,
                      pb=pb, col=col, dma_done=dma_done):
                    v.wait_ge(ctx.sems["dma"], dma_done)
                    v.tensor_tensor(
                        out=bmul[:],
                        in0=sp2[pb][:, kt * T:(kt + 1) * T],
                        in1=bbc[pb][:, n * T:(n + 1) * T], op=ALU.mult)
                    v.wait_ge(ctx.sems["s"], s_of[it])
                    ctx.inc(v.tensor_tensor_scan(
                        out=hb[:], data0=ab[:], data1=bmul[:],
                        initial=carry[:, col:col + 1],
                        op0=ALU.mult, op1=ALU.add), "vsc")
                    ctx.inc(v.tensor_tensor(
                        out=hcb[:], in0=hb[:],
                        in1=cbc[pb][:, n * T:(n + 1) * T],
                        op=ALU.mult), "v")
                    v.tensor_copy(out=carry[:, col:col + 1],
                                  in_=hb[:, T - 1:T])
                    if n == 0:
                        v.wait_ge(ctx.sems["odma"], ybuf_dma[kt % 2])
                        ctx.inc(v.tensor_copy(out=yg[kt % 2][:],
                                              in_=hcb[:]), "v")
                    else:
                        ctx.inc(v.tensor_tensor(
                            out=yg[kt % 2][:], in0=yg[kt % 2][:],
                            in1=hcb[:], op=ALU.add), "v")

                if (j + 1) % N_STATE == 0:
                    @blk.sync
                    def _(sync, kt=kt, sl=sl):
                        sync.wait_ge(ctx.sems["v"], ctx.cnt["v"])
                        ctx.inc(sync.dma_start(out=oyr[:, kt, sl],
                                               in_=yg[kt % 2][:]), "odma", 16)

                    ybuf_dma[kt % 2] = ctx.cnt["odma"]

            if j1 == KT * N_STATE:
                vend[ch] = ctx.cnt["v"]

        with nc.Block() as blk:
            for r in range(NCH + 2):
                if 0 <= r - 1 < NCH:
                    emit_A_post_p1(blk, r - 1)
                if r - 2 >= 0:
                    emit_scan_part(blk, r - 2, 0, 16)
                if 0 <= r - 1 < NCH:
                    emit_A_post_p2(blk, r - 1)
                if r - 2 >= 0:
                    emit_scan_part(blk, r - 2, 16, 48)
                if 0 <= r - 1 < NCH:
                    emit_A_post_p3(blk, r - 1)
                if r - 2 >= 0:
                    emit_scan_part(blk, r - 2, 48, 64)
                if r < NCH:
                    emit_A(blk, r)

        with nc.Block() as blk:
            @blk.sync
            def _(sync):
                ctx.wait(sync, "odma")
    return nc


_NC_AB = None


def _get_nc_AB():
    global _NC_AB
    if _NC_AB is None:
        _NC_AB = build_launch_AB()
    return _NC_AB


def run_AB(inputs):
    x_full = np.ascontiguousarray(inputs["L0_attn_x"], dtype=np.float32)
    alog = np.ascontiguousarray(inputs["L0_attn_A_log"], dtype=np.float32)
    xp = inputs["L0_attn_x_proj_w"]
    dtw = inputs["L0_attn_dt_proj_w"]
    w0 = inputs["L0_ffn_weight"]
    in_maps = []
    for c in range(NCORES):
        b, q = divmod(c, 4)
        ds = np.arange(DSL * q, DSL * (q + 1))
        rest = np.concatenate([np.arange(0, DSL * q),
                               np.arange(DSL * (q + 1), D)])
        perm = np.concatenate([ds, rest])
        xTp = np.ascontiguousarray(x_full[b].T[perm, :])
        xpTp = np.ascontiguousarray(xp.T[perm, :])
        w0Tp = np.ascontiguousarray(w0.T[perm, :][:, ds])
        in_maps.append(dict(
            xT=xTp, xpT=xpTp, w0Ts=w0Tp,
            dtTs=np.ascontiguousarray(dtw.T[:, ds]),
            dtbs=np.ascontiguousarray(
                inputs["L0_attn_dt_proj_b"][ds].reshape(KT_G, 128).T),
            b0s=np.ascontiguousarray(
                inputs["L0_ffn_bias"][ds].reshape(KT_G, 128).T),
            dpars=np.ascontiguousarray(
                inputs["L0_attn_D_param"][ds].reshape(KT_G, 128).T),
            alog=np.ascontiguousarray(alog[ds, :])))
    res = run_bass_kernel_spmd(_get_nc_AB(), in_maps,
                               core_ids=list(range(NCORES)))
    y = np.empty((B, L, D), np.float32)
    g0 = np.empty((B, L, D), np.float32)
    dxn = np.empty((B, L, D), np.float32)
    for c in range(NCORES):
        b, q = divmod(c, 4)
        ds = slice(DSL * q, DSL * (q + 1))
        y[b, :, ds] = res.results[c]["o_y"].T
        g0[b, :, ds] = res.results[c]["o_g0T"].T
        dxn[b, :, ds] = res.results[c]["o_dxnT"].T
    return y, g0, dxn


KT_G = DSL // 128


HF = RPC + 128
NTF = HF // 128
GS = 256
NEG = -1000000000.0


def build_launch_C():
    nc = bass.Bass("TRN2", target_bir_lowering=False, debug=False,
                   num_devices=NCORES)
    ctx = Ctx(nc)

    x1i = {}
    for nm in ("xh", "yh", "dxh", "g0h"):
        x1i[nm] = nc.dram_tensor(nm, [HF, D], F32, kind="ExternalInput").ap()
    wqT = nc.dram_tensor("wqT", [D, D], F32, kind="ExternalInput").ap()
    wkT = nc.dram_tensor("wkT", [D, D], F32, kind="ExternalInput").ap()
    wvT = nc.dram_tensor("wvT", [D, D], F32, kind="ExternalInput").ap()
    w1T = nc.dram_tensor("w1T", [D, D], F32, kind="ExternalInput").ap()
    bq_i = nc.dram_tensor("bq_i", [128, 16], F32, kind="ExternalInput").ap()
    bk_i = nc.dram_tensor("bk_i", [128, 16], F32, kind="ExternalInput").ap()
    bv_i = nc.dram_tensor("bv_i", [128, 16], F32, kind="ExternalInput").ap()
    b1_i = nc.dram_tensor("b1_i", [1, D], F32, kind="ExternalInput").ap()
    o_x2 = nc.dram_tensor("o_x2", [RPC, D], F32, kind="ExternalOutput").ap()
    x1_d = nc.dram_tensor("x1_d", [HF, D], F32, kind="Internal").ap()
    tau_d = nc.dram_tensor("tau_d", [1, D], F32, kind="Internal").ap()

    consts = ctx.sb("consts", [128, 8], F32)
    ident_bf = ctx.sb("ident_bf", [128, 128], BF16)
    ones_bf = ctx.sb("ones_bf", [128, 1], BF16)
    bq_t = ctx.sb("bq_t", [128, 16], F32)
    bqs_t = ctx.sb("bqs_t", [128, 16], F32)
    bk_t = ctx.sb("bk_t", [128, 16], F32)
    bv_t = ctx.sb("bv_t", [128, 16], F32)
    b1bc = ctx.sb("b1bc", [128, D], F32)
    taubc = ctx.sb("taubc", [128, D], F32)
    taurow = ctx.sb("taurow", [1, D], F32)
    mask0 = ctx.sb("mask0", [128, GS], F32)
    mask1 = ctx.sb("mask1", [128, GS], F32)
    xn1T = ctx.sb("xn1T", [128, NK * HF], BF16)
    s1bf = ctx.sb("s1bf", [128, NT * D], BF16)
    xt = ctx.sb("xt", [128, D], F32)
    xnf = ctx.sb("xnf", [128, D], F32)
    xnbf_tm = ctx.sb("xnbf_tm", [128, D], BF16)
    stat = ctx.sb("stat", [128, 16], F32)
    sp1 = ctx.sb("sp1", [128, D], F32)
    sp2 = ctx.sb("sp2", [128, D], F32)
    spb = ctx.sb("spb", [128, D], BF16)

    ps_g = [ctx.ps(f"ps_g{j}", [128, 512], F32) for j in range(2)]
    ps_sc = [ctx.ps(f"ps_sc{j}", [128, GS], F32) for j in range(2)]
    ps_tr = [ctx.ps(f"ps_tr{j}", [128, 128], BF16) for j in range(2)]
    ps_at = [ctx.ps(f"ps_at{j}", [128, 128], F32) for j in range(2)]

    ATSC = 1.0 / math.sqrt(128.0)
    with ctx.es:
        for j, val in enumerate([EPS, -1.0, 1.0, 0.0, ATSC]):
            nc.const_aps.register(F32, val, consts[:, j:j + 1])

        with nc.Block() as blk:
            @blk.gpsimd
            def _(gp):
                for j, val in enumerate([EPS, -1.0, 1.0, 0.0, ATSC]):
                    gp.memset(consts[:, j:j + 1], val)
                gp.iota(ident_bf[:], [[1, 128]], base=0, channel_multiplier=-1,
                        allow_small_or_imprecise_dtypes=True)
                gp.tensor_scalar(out=ident_bf[:], in0=ident_bf[:], scalar1=0.0,
                                 scalar2=None, op0=ALU.is_equal)
                gp.iota(mask0[:], [[1, GS]], base=0, channel_multiplier=-1,
                        allow_small_or_imprecise_dtypes=True)
                gp.iota(mask1[:], [[1, GS]], base=-128, channel_multiplier=-1,
                        allow_small_or_imprecise_dtypes=True)
                for m in (mask0, mask1):
                    gp.tensor_scalar(out=m[:], in0=m[:], scalar1=0.0,
                                     scalar2=NEG, op0=ALU.is_gt, op1=ALU.mult)

            @blk.sync
            def _(sync):
                ctx.inc(sync.dma_start(out=bq_t[:], in_=bq_i), "dma", 16)
                ctx.inc(sync.dma_start(out=bk_t[:], in_=bk_i), "dma", 16)
                ctx.inc(sync.dma_start(out=bv_t[:], in_=bv_i), "dma", 16)
                ctx.inc(sync.dma_start(out=b1bc[:],
                                       in_=b1_i.partition_broadcast(128)),
                        "dma", 16)

            @blk.vector
            def _(v):
                ctx.inc(v.memset(ones_bf[:], 1.0), "v")
                v.memset(taurow[0:1, :], 0.0)
                ctx.wait(v, "dma")
                ctx.inc(v.tensor_scalar(out=bqs_t[:], in0=bq_t[:],
                                        scalar1=ATSC, scalar2=None,
                                        op0=ALU.mult), "v")

            tau_s_done = {}
            for k in range(NK):
                wb = xt if k % 2 == 0 else xnf

                @blk.sync
                def _(sync, k=k, wb=wb):
                    if k >= 2:
                        sync.wait_ge(ctx.sems["s"], tau_s_done[k - 2])
                    ctx.inc(sync.dma_start(out=wb[:],
                                           in_=w1T[128 * k:128 * (k + 1), :]),
                            "dma", 16)

                tld = ctx.cnt["dma"]

                @blk.scalar
                def _(sc, k=k, wb=wb, tld=tld):
                    sc.wait_ge(ctx.sems["dma"], tld)
                    if k >= 1:
                        ctx.wait(sc, "t")
                    ctx.inc(sc.activation(out=spb[:], in_=wb[:], func=AF.Abs),
                            "s")
                    ctx.inc(sc.activation(out=stat[:, 10:11], in_=wb[:, 0:1],
                                          func=AF.Abs), "s")

                tau_s_done[k] = ctx.cnt["s"]

                for rnd in range(2):
                    @blk.tensor
                    def _(t, k=k, rnd=rnd):
                        ctx.wait(t, "s")
                        ctx.wait(t, "v")
                        for j in (2 * rnd, 2 * rnd + 1):
                            ctx.inc(t.matmul(ps_g[j % 2][0:1, :], ones_bf[:],
                                             spb[:, 512 * j:512 * (j + 1)],
                                             start=True, stop=True), "t")

                    @blk.vector
                    def _(v, k=k, rnd=rnd):
                        ctx.wait(v, "t")
                        for j in (2 * rnd, 2 * rnd + 1):
                            ctx.inc(v.tensor_tensor(
                                out=taurow[0:1, 512 * j:512 * (j + 1)],
                                in0=taurow[0:1, 512 * j:512 * (j + 1)],
                                in1=ps_g[j % 2][0:1, :], op=ALU.add), "v")

            @blk.vector
            def _(v):
                ctx.wait(v, "v")
                for j in range(4):
                    ctx.inc(v.tensor_scalar(
                        out=taurow[0:1, 512 * j:512 * (j + 1)],
                        in0=taurow[0:1, 512 * j:512 * (j + 1)],
                        scalar1=0.7 / D, scalar2=None, op0=ALU.mult), "v")

            @blk.sync
            def _(sync):
                ctx.wait(sync, "v")
                ctx.inc(sync.dma_start(out=tau_d, in_=taurow[0:1, :]), "dma", 16)
                ctx.wait(sync, "dma")
                ctx.inc(sync.dma_start(out=taubc[:],
                                       in_=tau_d.partition_broadcast(128)),
                        "dma", 16)
                ctx.wait(sync, "dma")

        es_res = ExitStack()
        yt = es_res.enter_context(nc.sbuf_tensor("yt", [128, D], F32))
        dxt = es_res.enter_context(nc.sbuf_tensor("dxt", [128, D], F32))
        g0t = es_res.enter_context(nc.sbuf_tensor("g0t", [128, D], F32))
        for i in range(NTF):
            with nc.Block() as blk:
                ctx.sem("odma")
                odma_prev = ctx.cnt["odma"]

                @blk.sync
                def _(sync, i=i, odma_prev=odma_prev):
                    rows = slice(128 * i, 128 * (i + 1))
                    sync.wait_ge(ctx.sem("odma"), odma_prev)
                    ctx.inc(sync.dma_start(out=xt[:], in_=x1i["xh"][rows, :]),
                            "dma", 16)
                    ctx.inc(sync.dma_start(out=yt[:], in_=x1i["yh"][rows, :]),
                            "dma", 16)
                    ctx.inc(sync.dma_start(out=dxt[:], in_=x1i["dxh"][rows, :]),
                            "dma", 16)
                    ctx.inc(sync.dma_start(out=g0t[:], in_=x1i["g0h"][rows, :]),
                            "dma", 16)

                @blk.vector
                def _(v, odma_prev=odma_prev):
                    ctx.wait(v, "dma")
                    v.wait_ge(ctx.sem("odma"), odma_prev)
                    ctx.inc(v.tensor_tensor(out=yt[:], in0=yt[:], in1=dxt[:],
                                            op=ALU.add), "v")
                    ctx.wait(v, "v")
                    ctx.inc(v.tensor_tensor(out=dxt[:], in0=yt[:], in1=g0t[:],
                                            op=ALU.mult), "v")
                    ctx.wait(v, "v")
                    ctx.inc(v.tensor_tensor(out=xt[:], in0=xt[:], in1=dxt[:],
                                            op=ALU.add), "v")

                vx1 = ctx.cnt["v"]

                @blk.sync
                def _(sync, i=i):
                    rows = slice(128 * i, 128 * (i + 1))
                    sync.wait_ge(ctx.sems["v"], vx1)
                    ctx.inc(sync.dma_start(out=x1_d[rows, :], in_=xt[:]),
                            "odma", 16)

                emit_ln_stats_apply(ctx, blk, xt, xnf, stat, sp1)
                vxn = ctx.cnt["v"]

                @blk.vector
                def _(v):
                    v.wait_ge(ctx.sems["v"], vxn)
                    ctx.inc(v.tensor_copy(out=xnbf_tm[:], in_=xnf[:]), "v")

                vbf = ctx.cnt["v"]

                for k in range(NK):
                    @blk.tensor
                    def _(t, k=k, i=i):
                        t.wait_ge(ctx.sems["v"], vbf)
                        if k >= 2:
                            ctx.wait(t, "vev", ctx.cnt["vev"] - 1)
                        ctx.inc(t.transpose(ps_tr[k % 2][:],
                                            xnbf_tm[:, 128 * k:128 * (k + 1)],
                                            ident_bf[:]), "t")

                    @blk.vector
                    def _(v, k=k, i=i):
                        ctx.wait(v, "t")
                        ctx.inc(v.tensor_copy(out=xn1T[:, HF * k + 128 * i:
                                                       HF * k + 128 * (i + 1)],
                                              in_=ps_tr[k % 2][:]), "vev")

        es_res.close()

        es_att = ExitStack()
        qTh = [es_att.enter_context(nc.sbuf_tensor(f"qTh{p}", [128, HF], BF16))
               for p in range(2)]
        kTh = [es_att.enter_context(nc.sbuf_tensor(f"kTh{p}", [128, HF], BF16))
               for p in range(2)]
        vTh = [es_att.enter_context(nc.sbuf_tensor(f"vTh{p}", [128, HF], BF16))
               for p in range(2)]
        vtmh = [es_att.enter_context(nc.sbuf_tensor(f"vtmh{p}",
                                                    [128, NTF * 128], BF16))
                for p in range(2)]
        wbf = [[es_att.enter_context(nc.sbuf_tensor(f"wbf{p}_{m}",
                                                    [128, NK * 128], BF16))
                for m in range(3)] for p in range(2)]
        scb = [es_att.enter_context(nc.sbuf_tensor(f"scb{p}", [128, GS], F32))
               for p in range(2)]
        ex = [es_att.enter_context(nc.sbuf_tensor(f"ex{p}", [128, GS], F32))
              for p in range(2)]
        attnb = [es_att.enter_context(nc.sbuf_tensor(f"attnb{p}", [128, GS],
                                                     BF16)) for p in range(2)]
        attnT = [es_att.enter_context(nc.sbuf_tensor(f"attnT{p}", [128, 256],
                                                     BF16)) for p in range(2)]
        srow = [es_att.enter_context(nc.sbuf_tensor(f"srow{p}", [128, 4], F32))
                for p in range(2)]

        units_unshift = []
        for g in range(4):
            u0 = 128 + GS * g
            units_unshift += [(u0, u0, GS, 0), (u0 + 128, u0, GS, 1)]
        units_shift = [(128, 0, GS, 1)]
        for g in range(3):
            u0 = GS * (g + 1)
            units_shift += [(u0, u0, GS, 0), (u0 + 128, u0, GS, 1)]
        units_shift.append((1024, 1024, 128, 0))

        with nc.Block() as blk:
            tmm_qkv = {}
            qkv_done_s = {}
            psg_last_s = {0: 0, 1: 0}
            ptr_last_s = {0: 0, 1: 0}

            def emit_load_w(h):
                hp = h % 2

                @blk.gpsimd
                def _(gp, h=h, hp=hp):
                    if h >= 2:
                        gp.wait_ge(ctx.sems["t"], tmm_qkv[h - 2])
                    for m, wT in enumerate((wqT, wkT, wvT)):
                        ctx.inc(gp.dma_start(
                            out=wbf[hp][m][:].rearrange("p (k o) -> p k o",
                                                        k=NK),
                            in_=wT[:, 128 * h:128 * (h + 1)].rearrange(
                                "(k p) o -> p k o", p=128)), "gdma", 16)
                return ctx.cnt["gdma"]

            def qkv_actions(h, gdma_done):
                hp = h % 2
                acts = []
                for src_i, (outbuf, bias, kw) in enumerate((
                        (qTh[hp], bqs_t, dict(scale=ATSC)),
                        (kTh[hp], bk_t, {}),
                        (vTh[hp], bv_t, {}))):
                    for cchunk in range(3):
                        t0, t1 = 384 * cchunk, 384 * (cchunk + 1)
                        g_i = src_i * 3 + cchunk

                        def act(h=h, hp=hp, src_i=src_i, t0=t0, t1=t1,
                                g_i=g_i, outbuf=outbuf, bias=bias, kw=kw,
                                gdma_done=gdma_done):
                            pj = g_i % 2
                            sev = {}

                            @blk.tensor
                            def _(t):
                                t.wait_ge(ctx.sems["gdma"], gdma_done)
                                t.wait_ge(ctx.sems["v"], psg_last_s[pj])
                                for k in range(NK):
                                    ctx.inc(t.matmul(
                                        ps_g[pj][:, 0:t1 - t0],
                                        wbf[hp][src_i][:, 128 * k:
                                                       128 * (k + 1)],
                                        xn1T[:, HF * k + t0:HF * k + t1],
                                        start=(k == 0), stop=(k == NK - 1)),
                                        "t")
                                tmm_qkv[h] = ctx.cnt["t"]

                            @blk.vector
                            def _(v):
                                ctx.wait(v, "t")
                                if kw:
                                    ctx.inc(v.tensor_scalar(
                                        out=outbuf[:, t0:t1],
                                        in0=ps_g[pj][:, 0:t1 - t0],
                                        scalar1=ATSC,
                                        scalar2=bias[:, h:h + 1],
                                        op0=ALU.mult, op1=ALU.add), "v")
                                else:
                                    ctx.inc(v.tensor_scalar(
                                        out=outbuf[:, t0:t1],
                                        in0=ps_g[pj][:, 0:t1 - t0],
                                        scalar1=bias[:, h:h + 1],
                                        scalar2=None, op0=ALU.add), "v")
                                psg_last_s[pj] = ctx.cnt["v"]
                        acts.append(act)
                for i in range(NTF):
                    def act(i=i, h=h, hp=hp):
                        @blk.tensor
                        def _(t):
                            t.wait_ge(ctx.sems["v"], ptr_last_s[i % 2])
                            t.wait_ge(ctx.sems["v"], psg_last_s[0])
                            t.wait_ge(ctx.sems["v"], psg_last_s[1])
                            ctx.inc(t.transpose(
                                ps_tr[i % 2][:],
                                vTh[hp][:, 128 * i:128 * (i + 1)],
                                ident_bf[:]), "t")
                            tmm_qkv[h] = ctx.cnt["t"]

                        @blk.vector
                        def _(v):
                            ctx.wait(v, "t")
                            ctx.inc(v.tensor_copy(
                                out=vtmh[hp][:, 128 * i:128 * (i + 1)],
                                in_=ps_tr[i % 2][:]), "v")
                            ptr_last_s[i % 2] = ctx.cnt["v"]
                            qkv_done_s[h] = ctx.cnt["v"]
                    acts.append(act)
                return acts

            def attn_actions(h):
                hp = h % 2
                hj = units_unshift if h < 8 else units_shift
                J = len(hj)
                st = dict(score={}, mx={}, exp={}, scale={}, trans={},
                          evT={}, mm={}, out={})
                acts = []
                for step in range(J + 2):
                    def act(step=step, hj=hj, J=J, st=st, hp=hp, h=h):
                        if step < J:
                            jj = step
                            (qs, kv0, kvl, ms) = hj[jj]
                            p = jj % 2

                            @blk.tensor
                            def _(t, jj=jj, p=p, qs=qs, kv0=kv0, kvl=kvl):
                                t.wait_ge(ctx.sems["v"], qkv_done_s[h])
                                if jj >= 2:
                                    t.wait_ge(ctx.sems["v"], st["mx"][jj - 2])
                                ctx.inc(t.matmul(ps_sc[p][:, 0:kvl],
                                                 qTh[hp][:, qs:qs + 128],
                                                 kTh[hp][:, kv0:kv0 + kvl],
                                                 start=True, stop=True), "t")
                                st["score"][jj] = ctx.cnt["t"]

                            @blk.vector
                            def _(v, jj=jj, p=p, kvl=kvl, ms=ms):
                                v.wait_ge(ctx.sems["t"], st["score"][jj])
                                if jj >= 2:
                                    v.wait_ge(ctx.sems["s"], st["exp"][jj - 2])
                                mask = [mask0, mask1][ms]
                                v.tensor_tensor(out=scb[p][:, 0:kvl],
                                                in0=ps_sc[p][:, 0:kvl],
                                                in1=mask[:, 0:kvl], op=ALU.add)
                                ctx.inc(v.tensor_reduce(out=srow[p][:, 0:1],
                                                        in_=scb[p][:, 0:kvl],
                                                        axis=AX.X, op=ALU.max,
                                                        negate=True), "v")
                                st["mx"][jj] = ctx.cnt["v"]

                            @blk.scalar
                            def _(sc, jj=jj, p=p, kvl=kvl):
                                sc.wait_ge(ctx.sems["v"], st["mx"][jj])
                                if jj >= 2 and (jj - 2) in st["scale"]:
                                    sc.wait_ge(ctx.sems["v"],
                                               st["scale"][jj - 2])
                                ctx.inc(sc.activation(out=ex[p][:, 0:kvl],
                                                      in_=scb[p][:, 0:kvl],
                                                      func=AF.Exp,
                                                      bias=srow[p][:, 0:1]),
                                        "s")
                                st["exp"][jj] = ctx.cnt["s"]

                        if 1 <= step and step - 1 < J:
                            jj = step - 1
                            (qs, kv0, kvl, ms) = hj[jj]
                            p = jj % 2

                            @blk.vector
                            def _(v, jj=jj, p=p, kvl=kvl):
                                v.wait_ge(ctx.sems["s"], st["exp"][jj])
                                if jj >= 2:
                                    kk = (jj - 2,
                                          1 if (jj - 2, 1) in st["trans"]
                                          else 0)
                                    v.wait_ge(ctx.sems["t"], st["trans"][kk])
                                ctx.inc(v.tensor_reduce(out=srow[p][:, 1:2],
                                                        in_=ex[p][:, 0:kvl],
                                                        axis=AX.X,
                                                        op=ALU.add), "v")
                                ctx.wait(v, "v")
                                ctx.inc(v.reciprocal(out=srow[p][:, 2:3],
                                                     in_=srow[p][:, 1:2]),
                                        "v")
                                ctx.wait(v, "v")
                                ctx.inc(v.tensor_scalar(
                                    out=attnb[p][:, 0:kvl],
                                    in0=ex[p][:, 0:kvl],
                                    scalar1=srow[p][:, 2:3], scalar2=None,
                                    op0=ALU.mult), "v")
                                st["scale"][jj] = ctx.cnt["v"]

                            nkv = kvl // 128
                            for kj in range(nkv):
                                @blk.tensor
                                def _(t, jj=jj, p=p, kj=kj):
                                    t.wait_ge(ctx.sems["v"], st["scale"][jj])
                                    t.wait_ge(ctx.sems["v"],
                                              ptr_last_s[kj % 2])
                                    ctx.inc(t.transpose(
                                        ps_tr[kj % 2][:],
                                        attnb[p][:, 128 * kj:128 * (kj + 1)],
                                        ident_bf[:]), "t")
                                    st["trans"][(jj, kj)] = ctx.cnt["t"]

                                @blk.vector
                                def _(v, jj=jj, p=p, kj=kj):
                                    v.wait_ge(ctx.sems["t"],
                                              st["trans"][(jj, kj)])
                                    ctx.inc(v.tensor_copy(
                                        out=attnT[p][:,
                                                    128 * kj:128 * (kj + 1)],
                                        in_=ps_tr[kj % 2][:]), "v")
                                    ptr_last_s[kj % 2] = ctx.cnt["v"]
                            st["evT"][jj] = ctx.cnt["v"]

                        if step >= 2:
                            jj = step - 2
                            (qs, kv0, kvl, ms) = hj[jj]
                            p = jj % 2
                            nkv = kvl // 128

                            @blk.tensor
                            def _(t, jj=jj, p=p, kv0=kv0, nkv=nkv):
                                t.wait_ge(ctx.sems["v"], st["evT"][jj])
                                if jj >= 2:
                                    t.wait_ge(ctx.sems["v"], st["out"][jj - 2])
                                for kj in range(nkv):
                                    ctx.inc(t.matmul(
                                        ps_at[p][:],
                                        attnT[p][:, 128 * kj:128 * (kj + 1)],
                                        vtmh[hp][:, kv0 + 128 * kj:
                                                 kv0 + 128 * (kj + 1)],
                                        start=(kj == 0), stop=(kj == nkv - 1)),
                                        "t")
                                st["mm"][jj] = ctx.cnt["t"]

                            @blk.vector
                            def _(v, jj=jj, p=p, qs=qs):
                                v.wait_ge(ctx.sems["t"], st["mm"][jj])
                                orow = qs - 128
                                oc = (orow // 128) * D + 128 * h
                                ctx.inc(v.tensor_copy(
                                    out=s1bf[:, oc:oc + 128],
                                    in_=ps_at[p][:]), "v")
                                st["out"][jj] = ctx.cnt["v"]
                    acts.append(act)
                return acts

            prev_attn = []
            for h in range(16):
                gdma_done = emit_load_w(h)
                qa = qkv_actions(h, gdma_done)
                n = max(len(qa), len(prev_attn))
                for s in range(n):
                    if s < len(qa):
                        qa[s]()
                    if s < len(prev_attn):
                        prev_attn[s]()
                prev_attn = attn_actions(16 - 1) if False else attn_actions(h)
            for act in prev_attn:
                act()

        es_att.close()

        qw1 = ctx.sb("qw1", [128, NK * D], BF16)
        with nc.Block() as blk:
            q_s_done = {}
            q_v_done = {}
            for k in range(NK):
                wb = xt if k % 2 == 0 else xnf
                ab = sp1 if k % 2 == 0 else sp2
                sg = spb if k % 2 == 0 else xnbf_tm

                @blk.sync
                def _(sync, k=k, wb=wb):
                    if k >= 2:
                        sync.wait_ge(ctx.sems["s"], q_s_done[k - 2])
                    ctx.inc(sync.dma_start(out=wb[:],
                                           in_=w1T[128 * k:128 * (k + 1), :]),
                            "dma", 16)

                qld = ctx.cnt["dma"]

                @blk.scalar
                def _(sc, k=k, wb=wb, ab=ab, sg=sg, qld=qld):
                    sc.wait_ge(ctx.sems["dma"], qld)
                    if k >= 2:
                        sc.wait_ge(ctx.sems["v"], q_v_done[k - 2])
                    ctx.inc(sc.activation(out=ab[:], in_=wb[:], func=AF.Abs),
                            "s")
                    ctx.inc(sc.activation(out=sg[:], in_=wb[:], func=AF.Sign),
                            "s")

                q_s_done[k] = ctx.cnt["s"]

                @blk.vector
                def _(v, k=k, ab=ab, sg=sg):
                    ctx.wait(v, "s")
                    ctx.inc(v.tensor_tensor(out=ab[:], in0=ab[:],
                                            in1=taubc[:], op=ALU.is_gt), "v")
                    ctx.wait(v, "v")
                    ctx.inc(v.tensor_tensor(out=qw1[:, D * k:D * (k + 1)],
                                            in0=ab[:], in1=sg[:],
                                            op=ALU.mult), "v")

                q_v_done[k] = ctx.cnt["v"]

        with nc.Block() as blk:
            ev_done = {}
            sg1_by = {}
            for i in range(NT):
                fi = i + 1
                for j in range(4):
                    gidx = i * 4 + j
                    pb = gidx % 2

                    @blk.tensor
                    def _(t, fi=fi, j=j, pb=pb, gidx=gidx):
                        if gidx < 2:
                            ctx.wait(t, "v")
                        else:
                            t.wait_ge(ctx.sems["v"], ev_done[gidx - 2])
                        for k in range(NK):
                            ctx.inc(t.matmul(
                                ps_g[pb][:],
                                xn1T[:, HF * k + 128 * fi:
                                     HF * k + 128 * (fi + 1)],
                                qw1[:, D * k + 512 * j:D * k + 512 * (j + 1)],
                                start=(k == 0), stop=(k == NK - 1)), "t")

                    tgrp = ctx.cnt["t"]

                    @blk.vector
                    def _(v, j=j, pb=pb, tgrp=tgrp, gidx=gidx, i=i):
                        v.wait_ge(ctx.sems["t"], tgrp)
                        if i >= 1:
                            v.wait_ge(ctx.sems["s"], sg1_by[i - 1])
                        sl = slice(512 * j, 512 * (j + 1))
                        ctx.inc(v.tensor_tensor(out=sp1[:, sl],
                                                in0=ps_g[pb][:],
                                                in1=b1bc[:, sl], op=ALU.add),
                                "v")
                        ev_done[gidx] = ctx.cnt["v"]

                vg1 = ctx.cnt["v"]

                @blk.sync
                def _(sync, fi=fi):
                    rows = slice(128 * fi, 128 * (fi + 1))
                    ctx.wait(sync, "odma")
                    ctx.inc(sync.dma_start(out=xt[:], in_=x1_d[rows, :]),
                            "dma", 16)

                @blk.scalar
                def _(sc, vg1=vg1):
                    sc.wait_ge(ctx.sems["v"], vg1)
                    ctx.inc(sc.activation(out=sp2[:], in_=sp1[:], func=AF.Silu),
                            "s")

                sg1 = ctx.cnt["s"]
                sg1_by[i] = sg1

                @blk.vector
                def _(v, i=i, sg1=sg1):
                    v.wait_ge(ctx.sems["s"], sg1)
                    ctx.wait(v, "dma")
                    ctx.inc(v.tensor_tensor(out=sp2[:], in0=sp2[:],
                                            in1=s1bf[:, D * i:D * (i + 1)],
                                            op=ALU.mult), "v")
                    ctx.wait(v, "v")
                    ctx.inc(v.tensor_tensor(out=xt[:], in0=xt[:], in1=sp2[:],
                                            op=ALU.add), "v")

                vx2 = ctx.cnt["v"]

                @blk.sync
                def _(sync, i=i, vx2=vx2):
                    rows = slice(128 * i, 128 * (i + 1))
                    sync.wait_ge(ctx.sems["v"], vx2)
                    ctx.inc(sync.dma_start(out=o_x2[rows, :], in_=xt[:]),
                            "odma", 16)

        with nc.Block() as blk:
            @blk.sync
            def _(sync):
                ctx.wait(sync, "odma")
    return nc


_NC_C = None


def _get_nc_C():
    global _NC_C
    if _NC_C is None:
        _NC_C = build_launch_C()
    return _NC_C


def run_C(inputs, y, g0, dxn):
    x_full = np.ascontiguousarray(inputs["L0_attn_x"], dtype=np.float32)
    wqT = np.ascontiguousarray(inputs["L1_attn_Wq"].T)
    wkT = np.ascontiguousarray(inputs["L1_attn_Wk"].T)
    wvT = np.ascontiguousarray(inputs["L1_attn_Wv"].T)
    w1T = np.ascontiguousarray(inputs["L1_ffn_weight"].T)
    bq = np.ascontiguousarray(inputs["L1_attn_bq"].reshape(16, 128).T)
    bk = np.ascontiguousarray(inputs["L1_attn_bk"].reshape(16, 128).T)
    bv = np.ascontiguousarray(inputs["L1_attn_bv"].reshape(16, 128).T)
    b1 = np.ascontiguousarray(inputs["L1_ffn_bias"].reshape(1, D))
    in_maps = []
    for c in range(NCORES):
        b, q = divmod(c, 4)
        idx = (np.arange(RPC * q - 128, RPC * (q + 1)) % L)
        in_maps.append(dict(
            xh=np.ascontiguousarray(x_full[b][idx]),
            yh=np.ascontiguousarray(y[b][idx]),
            dxh=np.ascontiguousarray(dxn[b][idx]),
            g0h=np.ascontiguousarray(g0[b][idx]),
            wqT=wqT, wkT=wkT, wvT=wvT, w1T=w1T,
            bq_i=bq, bk_i=bk, bv_i=bv, b1_i=b1))
    res = run_bass_kernel_spmd(_get_nc_C(), in_maps, core_ids=list(range(NCORES)))
    out = np.empty((B, L, D), np.float32)
    for c in range(NCORES):
        b, q = divmod(c, 4)
        out[b, RPC * q:RPC * (q + 1), :] = res.results[c]["o_x2"]
    return out


def kernel(**inputs):
    y, g0, dxn = run_AB(inputs)
    return run_C(inputs, y, g0, dxn)
